# version 02
# baseline (speedup 1.0000x reference)
import numpy as np
import ml_dtypes
import concourse.bass as bass
import concourse.mybir as mybir
from concourse.ap import AP
from concourse.bass_utils import run_bass_kernel_spmd

F32 = mybir.dt.float32
BF16 = mybir.dt.bfloat16
U8 = mybir.dt.uint8
AF = mybir.ActivationFunctionType
ALU = mybir.AluOpType
AX = mybir.AxisListType
f32 = np.float32
PI = np.pi

NCORE = 8
ROWS = 1024
SP = 512
NPAN = 2
NI = 86
NCC = 43
NCH = 16
D_MODEL = 4096
INTER = 11008
LL = 4096
NG4 = 8

SL = dict(lam_r=0, lam_i=64, lam_r2=128, BR=192, BI=256,
          v00R=320, v00I=384, v01R=448, v01I=512,
          v10R=576, v10I=640, v11R=704, v11I=768,
          vRn00=832, vRn01=896, vRn10=960, vRn11=1024,
          negr2=1088, scratch=1152)
RB_W = 1216
M_CLASSES = list(range(30, 62))
NMC = len(M_CLASSES)


class Ctr:
    def __init__(self, s, step=1):
        self.sem, self.step, self.v = s, step, 0

    def inc(self, ins):
        ins.then_inc(self.sem, self.step)
        self.v += self.step
        return self.v


class MultiCtr:
    CAP = 30000

    def __init__(self, mksem, name, step):
        self.mksem, self.name, self.step = mksem, name, step
        self.idx = 0
        self.sem = mksem(f"{name}0")
        self.v = 0

    def inc(self, ins):
        if self.v + self.step > self.CAP:
            self.idx += 1
            self.sem = self.mksem(f"{self.name}{self.idx}")
            self.v = 0
        ins.then_inc(self.sem, self.step)
        self.v += self.step
        return (self.sem, self.v)

    def ms(self):
        return (self.sem, self.v)


def WAIT(eng, ms):
    eng.wait_ge(ms[0], ms[1])


def build_program(ssm_on=True, ffn_on=True, dbg=False, upto=6):
    nc = bass.Bass("TRN2", target_bir_lowering=False)

    def inp(name, shape, dt=F32):
        return nc.dram_tensor(name, shape, dt, kind="ExternalInput").ap()

    x_in = inp("x", [ROWS, D_MODEL])
    xr_in = inp("xr", [ROWS, D_MODEL])
    wg_slab = inp("wg_slab", [NI, 128, 2048], U8)
    wu_slab = inp("wu_slab", [NI, 128, 2048], U8)
    wd_slab = inp("wd_slab", [NCC, 128, 4096], U8)
    sg_rev = inp("sg_rev", [128, 5504])
    su_rev = inp("su_rev", [128, 5504])
    zg_rev = inp("zg_rev", [128, 5504])
    zu_rev = inp("zu_rev", [128, 5504])
    sg_pack = inp("sg_pack", [128, 5504])
    su_pack = inp("su_pack", [128, 5504])
    sd_nat = inp("sd_nat", [128, 5504])
    zd_nat = inp("zd_nat", [128, 5504])
    pat1_in = inp("pat1", [NCH, 128, 64], BF16)
    pat17_in = inp("pat17", [NCH, 128, 64], BF16)
    qpatA_in = inp("qpatA", [32, 128, 128], BF16)
    qpatB_in = inp("qpatB", [11, 128, 44], BF16)
    ident_in = inp("ident", [128, 128], BF16)
    dvec_in = inp("dvec", [128, 1])
    ssm_names = [("lam_r", [1, 64]), ("lam_i", [1, 64]), ("p_r", [1, 64]),
                 ("p_i", [1, 64]), ("q_r", [1, 64]), ("q_i", [1, 64]),
                 ("ct_r", [1, 64]), ("ct_i", [1, 64]), ("vcr_t", [64, 64]),
                 ("vci_t", [64, 64]), ("bp", [64, 1]), ("log_step", [1, 1]),
                 ("lramp", [64, 64]), ("re", [64, 64]), ("re2", [64, 64]),
                 ("rw", [64, 64]), ("rw2", [64, 64])]
    ssm_in = {nm: inp("i_" + nm, shp) for nm, shp in ssm_names}

    y_out = nc.dram_tensor("y", [ROWS, D_MODEL], F32, kind="ExternalOutput").ap()
    if dbg:
        dbg_wB = nc.dram_tensor("dbg_wB", [128, 2048], BF16,
                                kind="ExternalOutput").ap()
        dbg_wH = nc.dram_tensor("dbg_wH", [128, 2048], BF16,
                                kind="ExternalOutput").ap()
        dbg_sc = nc.dram_tensor("dbg_sc", [128, 2048], BF16,
                                kind="ExternalOutput").ap()
        dbg_wBu = nc.dram_tensor("dbg_wBu", [128, 2048], BF16,
                                 kind="ExternalOutput").ap()
        dbg_wHu = nc.dram_tensor("dbg_wHu", [128, 2048], BF16,
                                 kind="ExternalOutput").ap()
        dbg_scu = nc.dram_tensor("dbg_scu", [128, 2048], BF16,
                                 kind="ExternalOutput").ap()
        dbg_zsu = nc.dram_tensor("dbg_zsu", [64, 128], BF16,
                                 kind="ExternalOutput").ap()

    kbuf = nc.dram_tensor("kbuf", [8192], BF16, kind="Internal").ap()
    k2buf = nc.dram_tensor("k2buf", [8192], BF16, kind="Internal").ap()
    sgb = nc.dram_tensor("sgb", [704512], BF16, kind="Internal").ap()
    sub = nc.dram_tensor("sub", [704512], BF16, kind="Internal").ap()
    sdb = nc.dram_tensor("sdb", [704512], BF16, kind="Internal").ap()
    zgb = nc.dram_tensor("zgb", [704512], BF16, kind="Internal").ap()
    zub = nc.dram_tensor("zub", [704512], BF16, kind="Internal").ap()
    zdb = nc.dram_tensor("zdb", [704512], BF16, kind="Internal").ap()

    from contextlib import ExitStack
    with ExitStack() as ctx:
        def sbt(name, shape, dt=F32):
            return ctx.enter_context(nc.sbuf_tensor(name, shape, dt))

        def psm(name, shape, dt=F32):
            return ctx.enter_context(nc.psum_tensor(name, shape, dt))

        def sem(name):
            return ctx.enter_context(nc.semaphore(name))

        h_act = sbt("h_act", [128, NI * 512], BF16)
        xsT = sbt("xsT", [128, NCH * 512], BF16)
        xdT = sbt("xdT", [128, NCH * 512], BF16)
        s_id = sbt("s_id", [128, 128], BF16)
        s_dvec = sbt("s_dvec", [128, 1])
        s_stats = sbt("s_stats", [128, 24])
        s_t = sbt("s_t", [128, 16])
        s_hsumA = sbt("s_hsumA", [128, 512], BF16)
        s_hsumB = sbt("s_hsumB", [64, 512], BF16)
        s_xsum = sbt("s_xsum", [64, 512], BF16)
        s_sig = [sbt(f"s_sig{j}", [128, 512], BF16) for j in range(2)]
        s_h1 = [sbt(f"s_h1{j}", [128, 512], BF16) for j in range(2)]
        s_zs = {m: [sbt(f"s_zs_{m}{j}", [64, 128], BF16) for j in range(2)]
                for m in "gu"}
        s_kb = sbt("s_kb", [64, 64], BF16)
        s_zt = sbt("s_zt", [64, 64], BF16)
        s_kr1 = sbt("s_kr1", [64, 64], BF16)
        s_kr2 = sbt("s_kr2", [64, 64], BF16)
        s_k2t = sbt("s_k2t", [64, 64], BF16)
        k_ones = sbt("k_ones", [1, 64])
        s_m15 = sbt("s_m15", [128, 1], mybir.dt.uint32)
        k_scal = sbt("k_scal", [1, 8])
        s_ssm = {nm: sbt("si_" + nm, shp) for nm, shp in ssm_names}

        AR_B = 69632
        arena = sbt("arena", [128, AR_B], U8)

        def carve(off, parts, n, dt):
            esz = {F32: 4, BF16: 2, U8: 1}[dt]
            assert off % 64 == 0 and off + n * esz <= AR_B, (off, n)
            return arena[0:parts, off:off + n * esz].bitcast(dt)

        knames = ["s1", "s2", "sg", "clamped", "rec0", "T", "gi",
                  "dI", "sq", "den", "rec", "u", "w",
                  "k00R", "k00I", "k01R", "k01I", "k10R", "k10I",
                  "k11R", "k11I", "t1", "t2", "t3", "t4", "arR", "arI",
                  "ERc", "ERs", "ERsn", "Wc", "Ws", "SR", "SI", "PR", "PI_",
                  "lrneg", "kf", "dIb", "sqb", "denb", "recb", "ub", "wb",
                  "p1", "p2", "p3", "p4", "q1", "q2", "q3", "q4",
                  "den2", "nR", "nI", "x1", "x2", "pad",
                  "k00Rw", "k00Iw", "k01Rw", "k01Iw",
                  "k10Rw", "k10Iw", "k11Rw", "k11Iw"]
        KW = {nm: carve(i * 256, 64, 64, F32) for i, nm in enumerate(knames)}
        KOFF = len(knames) * 256
        k_repl = carve(KOFF, 64, RB_W, F32)
        k_rowbuf = carve(KOFF + 4864, 1, RB_W, F32)
        cvtA = carve(26368, 128, 1376, F32)
        cvtB = carve(31872, 128, 1376, F32)
        cvtO = carve(37376, 128, 1376, BF16)

        a_xr = [carve(0, 128, D_MODEL, F32), carve(16384, 128, D_MODEL, F32)]
        a_xnb = [carve(32768, 128, D_MODEL, BF16),
                 carve(40960, 128, D_MODEL, BF16)]
        a_xd2 = [carve(49152, 128, 2048, BF16), carve(53248, 128, 2048, BF16)]
        a_lnsq = carve(57344, 128, D_MODEL, BF16)
        a_pat1 = carve(65536, 128, NCH * 64, BF16)
        a_pat17 = carve(67584, 128, NCH * 64, BF16)

        b_wq = {"g": [carve(0, 128, 2048, U8), carve(2048, 128, 2048, U8)],
                "u": [carve(4096, 128, 2048, U8), carve(6144, 128, 2048, U8)]}
        b_sc = {"g": [carve(8192, 128, 2048, BF16), carve(12288, 128, 2048, BF16)],
                "u": [carve(16384, 128, 2048, BF16), carve(20480, 128, 2048, BF16)]}
        b_lo = {"g": carve(24576, 128, 2048, U8),
                "u": carve(26624, 128, 2048, U8)}
        b_hic = {"g": [carve(28672, 128, 2048, U8), carve(30720, 128, 2048, U8)],
                 "u": [carve(32768, 128, 2048, U8), carve(34816, 128, 2048, U8)]}
        b_wB = {"g": [carve(36864, 128, 2048, BF16), carve(40960, 128, 2048, BF16)],
                "u": [carve(45056, 128, 2048, BF16), carve(49152, 128, 2048, BF16)]}
        b_wH = {"g": [carve(53248, 128, 2048, BF16), carve(57344, 128, 2048, BF16)],
                "u": [carve(61440, 128, 2048, BF16), carve(65536, 128, 2048, BF16)]}

        c_qpA = carve(0, 128, 32 * 128, BF16)
        c_qpB = carve(8192, 128, 11 * 44, BF16)
        c_md = carve(9216, 128, NMC * 128, BF16)
        c_ms = carve(17664, 128, NMC * 128, BF16)
        c_zdA = carve(26112, 128, D_MODEL, BF16)
        c_zdB = carve(34304, 44, D_MODEL, BF16)
        c_wqd = [carve(42496, 128, 512, U8), carve(43008, 128, 512, U8)]
        c_scd = [carve(43520, 128, 512, BF16), carve(44544, 128, 512, BF16)]
        c_lo = carve(45568, 128, 512, U8)
        c_hic = [carve(46080, 128, 512, U8), carve(46592, 128, 512, U8)]
        c_wBd = [carve(48640, 128, 512, BF16), carve(49664, 128, 512, BF16)]
        c_wHd = [carve(50688, 128, 512, BF16), carve(51712, 128, 512, BF16)]
        c_yT = [carve(52736 + 1024 * j, 128, 512, BF16) for j in range(4)]
        c_xa = [carve(56832, 128, 512, F32), carve(58880, 128, 512, F32)]
        c_xna = [carve(60928, 128, 512, F32), carve(62976, 128, 512, F32)]
        c_oa = [carve(65024, 128, 512, F32), carve(67072, 128, 512, F32)]

        p_big = psm("p_big", [128, 3072], F32)
        p_trb = psm("p_trb", [128, 1024], BF16)
        p_trc = psm("p_trc", [128, 1024], BF16)
        kp_a = p_big[0:64, 0:64]
        kp_b = p_big[0:64, 512:576]
        kp_r1 = p_big[0:1, 1024:1088]
        kp_repl = p_big[0:64, 1536:2048]
        p_xsum = p_big[0:64, 2048:2560]
        p_hA = p_big[:, 2560:3072]
        p_hB = p_big[0:44, 2048:2560]

        cW = MultiCtr(sem, "sW", 16)
        cX = Ctr(sem("sX"), 16)
        cO = Ctr(sem("sO"), 16)
        cS = Ctr(sem("sS"))
        cV = Ctr(sem("sV"))
        cG = Ctr(sem("sG"))
        cT = Ctr(sem("sT"))
        kv = Ctr(sem("kv")); kt = Ctr(sem("kt")); kg = Ctr(sem("kg"))
        dbsem = sem("dbsem")
        ko = Ctr(sem("ko"), 16); kd = Ctr(sem("kd"), 16); kq = Ctr(sem("kq"))

        block = ctx.enter_context(nc.Block())
        MS = {}

        def sy(fn): block.sync(fn)
        def ve(fn): block.vector(fn)
        def sc(fn): block.scalar(fn)
        def gp(fn): block.gpsimd(fn)
        def te(fn): block.tensor(fn)

        SGC = dict(skip_group_check=True)

        def _init_small(sync):
            for nm, _s in ssm_names:
                cW.inc(sync.dma_start(out=s_ssm[nm][:], in_=ssm_in[nm]))
            cW.inc(sync.dma_start(out=s_id[:], in_=ident_in))
            cW.inc(sync.dma_start(out=s_dvec[:], in_=dvec_in))
        sy(_init_small)
        MS["small"] = cW.ms()

        def _k_scalar(scalar):
            WAIT(scalar, MS["small"])
            scalar.activation(out=k_scal[:, 0:1], in_=s_ssm["log_step"][:],
                              func=AF.Exp, scale=np.float32(-1.0))
            kg.inc(scalar.activation(out=KW["s1"][:], in_=s_ssm["lramp"][:],
                                     func=AF.Sin, scale=np.float32(PI / LL)))
            scalar.activation(out=KW["s2"][:], in_=s_ssm["lramp"][:], func=AF.Sin,
                              scale=np.float32(PI / LL), bias=np.float32(-PI / 2))
            scalar.activation(out=KW["ERc"][:], in_=s_ssm["re2"][:], func=AF.Sin,
                              scale=np.float32(2 * PI / 64))
            scalar.activation(out=KW["ERs"][:], in_=s_ssm["re"][:], func=AF.Sin,
                              scale=np.float32(2 * PI / 64))
            scalar.activation(out=KW["ERsn"][:], in_=s_ssm["re"][:], func=AF.Sin,
                              scale=np.float32(-2 * PI / 64))
            scalar.activation(out=KW["Wc"][:], in_=s_ssm["rw2"][:], func=AF.Sin,
                              scale=np.float32(2 * PI / LL))
            kg.inc(scalar.activation(out=KW["Ws"][:], in_=s_ssm["rw"][:],
                                     func=AF.Sin, scale=np.float32(2 * PI / LL)))
        sc(_k_scalar)

        def _k_gp(gpsimd):
            WAIT(gpsimd, MS["small"])
            gpsimd.memset(k_ones[:], 1.0)
            kq.inc(gpsimd.memset(s_zt[:], 0.0))
        gp(_k_gp)

        def _k_tensor(tensor):
            WAIT(tensor, MS["small"])
            kt.inc(tensor.matmul(kp_r1, s_ssm["bp"][:], s_ssm["vcr_t"][:],
                                 start=True, stop=True))
            tensor.wait_ge(kv.sem, 1)
            kt.inc(tensor.matmul(kp_r1, s_ssm["bp"][:], s_ssm["vci_t"][:],
                                 start=True, stop=True))
            tensor.wait_ge(kv.sem, 3)
            tensor.wait_ge(kq.sem, 1)
            kt.inc(tensor.matmul(kp_repl, k_ones[:], k_rowbuf[:, 0:512],
                                 start=True, stop=True))
            tensor.wait_ge(kv.sem, 4)
            kt.inc(tensor.matmul(kp_repl, k_ones[:], k_rowbuf[:, 512:1024],
                                 start=True, stop=True))
            tensor.wait_ge(kv.sem, 5)
            kt.inc(tensor.matmul(kp_repl[:, 0:RB_W - 1024], k_ones[:],
                                 k_rowbuf[:, 1024:RB_W], start=True, stop=True))
            tensor.wait_ge(kv.sem, 7)
            tensor.matmul(kp_a, KW["arR"][:], KW["ERc"][:], start=True, stop=False)
            tensor.matmul(kp_a, KW["arI"][:], KW["ERsn"][:], start=False, stop=True)
            tensor.matmul(kp_b, KW["arR"][:], KW["ERs"][:], start=True, stop=False)
            kt.inc(tensor.matmul(kp_b, KW["arI"][:], KW["ERc"][:],
                                 start=False, stop=True))
            tensor.wait_ge(kv.sem, 9)
            tensor.matmul(kp_a, KW["ERc"][:], KW["PR"][:], start=True, stop=False)
            kt.inc(tensor.matmul(kp_a, KW["ERsn"][:], KW["PI_"][:],
                                 start=False, stop=True))
        te(_k_tensor)

        def _k_vector(vector):
            vector.memset(KW["pad"][:], 0.0)
            vector.memset(s_t[:], 0.0)
            vector.memset(s_m15[:], 15.0)
            WAIT(vector, MS["small"])
            vector.wait_ge(kt.sem, 1)
            kv.inc(vector.tensor_copy(out=k_rowbuf[:, SL["BR"]:SL["BR"] + 64],
                                      in_=kp_r1))
            vector.wait_ge(kt.sem, 2)
            kv.inc(vector.tensor_copy(out=k_rowbuf[:, SL["BI"]:SL["BI"] + 64],
                                      in_=kp_r1))
            BR = k_rowbuf[:, SL["BR"]:SL["BR"] + 64]
            BI = k_rowbuf[:, SL["BI"]:SL["BI"] + 64]
            lamr, lami = s_ssm["lam_r"][:], s_ssm["lam_i"][:]
            vector.tensor_copy(out=k_rowbuf[:, 0:64], in_=lamr)
            vector.tensor_copy(out=k_rowbuf[:, 64:128], in_=lami)
            vector.tensor_mul(k_rowbuf[:, 128:192], lamr, lamr)
            ctr, cti = s_ssm["ct_r"][:], s_ssm["ct_i"][:]
            pr, pi_ = s_ssm["p_r"][:], s_ssm["p_i"][:]
            qr, qi = s_ssm["q_r"][:], s_ssm["q_i"][:]
            t_a = k_rowbuf[:, SL["scratch"]:SL["scratch"] + 64]
            padt = KW["pad"]

            def vpad():
                vector.tensor_copy(out=padt[:, 0:64], in_=padt[:, 0:64])

            def cmulconj(dR, dI, xr_, xi_, yr_, yi_):
                vector.tensor_mul(dR, xr_, yr_)
                vector.tensor_mul(dI, xr_, yi_)
                vector.tensor_mul(t_a, xi_, yi_)
                vpad()
                vector.tensor_add(dR, dR, t_a)
                vector.tensor_mul(t_a, xi_, yr_)
                vpad()
                vector.tensor_sub(dI, dI, t_a)
            rb = lambda k: k_rowbuf[:, SL[k]:SL[k] + 64]
            cmulconj(rb("v00R"), rb("v00I"), ctr, cti, BR, BI)
            cmulconj(rb("v01R"), rb("v01I"), ctr, cti, pr, pi_)
            cmulconj(rb("v10R"), rb("v10I"), qr, qi, BR, BI)
            cmulconj(rb("v11R"), rb("v11I"), qr, qi, pr, pi_)
            for k in range(4):
                vector.tensor_scalar_mul(
                    rb(f"vRn{['00', '01', '10', '11'][k]}"),
                    rb(['v00R', 'v01R', 'v10R', 'v11R'][k]), -1.0)
            vector.wait_ge(kg.sem, 1)
            vector.memset(rb("negr2"), 0.0)
            vpad()
            kv.inc(vector.tensor_scalar(out=rb("negr2"), in0=rb("negr2"),
                                        scalar1=k_scal[:, 0:1], scalar2=-2.0,
                                        op0=ALU.add, op1=ALU.mult))
            vector.wait_ge(kt.sem, 3)
            kv.inc(vector.tensor_copy(out=k_repl[:, 0:512], in_=kp_repl))
            vector.wait_ge(kt.sem, 4)
            kv.inc(vector.tensor_copy(out=k_repl[:, 512:1024], in_=kp_repl))
            vector.wait_ge(kt.sem, 5)
            kv.inc(vector.tensor_copy(out=k_repl[:, 1024:RB_W],
                                      in_=kp_repl[:, 0:RB_W - 1024]))
            vector.wait_ge(kg.sem, 2)
            vector.tensor_scalar(out=KW["sg"][:], in0=KW["s2"][:], scalar1=0.0,
                                 scalar2=None, op0=ALU.is_ge)
            vector.tensor_scalar_mul(KW["lrneg"][:], k_repl[:, 0:64], -1.0)
            vector.tensor_scalar(out=KW["sg"][:], in0=KW["sg"][:], scalar1=2e-12,
                                 scalar2=-1e-12, op0=ALU.mult, op1=ALU.add)
            vpad()
            vector.tensor_add(KW["clamped"][:], KW["s2"][:], KW["sg"][:])
            vpad()
            vector.reciprocal(KW["rec0"][:], KW["clamped"][:])
            vpad()
            vector.tensor_mul(KW["T"][:], KW["s1"][:], KW["rec0"][:])
            vpad()
            vector.tensor_scalar_max(KW["T"][:], KW["T"][:], -1e9)
            vpad()
            vector.tensor_scalar_min(KW["T"][:], KW["T"][:], 1e9)
            vpad()
            vector.tensor_scalar(out=KW["gi"][:], in0=KW["T"][:],
                                 scalar1=k_repl[:, SL["negr2"]:SL["negr2"] + 1],
                                 scalar2=None, op0=ALU.mult)
            vpad()
            vector.tensor_scalar_mul(KW["T"][:], KW["T"][:], -1.0)
            for nm in ["k00R", "k00I", "k01R", "k01I", "k10R", "k10I",
                       "k11R", "k11I", "k00Rw", "k00Iw", "k01Rw", "k01Iw",
                       "k10Rw", "k10Iw", "k11Rw", "k11Iw"]:
                vector.memset(KW[nm][:], 0.0)
            pairs = [("k00R", "k00I", "v00R", "v00I", "vRn00"),
                     ("k01R", "k01I", "v01R", "v01I", "vRn01"),
                     ("k10R", "k10I", "v10R", "v10I", "vRn10"),
                     ("k11R", "k11I", "v11R", "v11I", "vRn11")]
            rc = lambda k, n: k_repl[:, SL[k] + n:SL[k] + n + 1]
            for n in range(0, 64, 2):
                m = n + 1
                vector.tensor_scalar(out=KW["dI"][:], in0=KW["gi"][:],
                                     scalar1=rc("lam_i", n), scalar2=None,
                                     op0=ALU.subtract)
                vector.tensor_scalar(out=KW["dIb"][:], in0=KW["gi"][:],
                                     scalar1=rc("lam_i", m), scalar2=None,
                                     op0=ALU.subtract)
                vector.tensor_mul(KW["sq"][:], KW["dI"][:], KW["dI"][:])
                vector.tensor_mul(KW["sqb"][:], KW["dIb"][:], KW["dIb"][:])
                vector.tensor_scalar(out=KW["den"][:], in0=KW["sq"][:],
                                     scalar1=rc("lam_r2", n), scalar2=None,
                                     op0=ALU.add)
                vector.tensor_scalar(out=KW["denb"][:], in0=KW["sqb"][:],
                                     scalar1=rc("lam_r2", m), scalar2=None,
                                     op0=ALU.add)
                vector.reciprocal(KW["rec"][:], KW["den"][:])
                vector.reciprocal(KW["recb"][:], KW["denb"][:])
                vector.tensor_scalar(out=KW["u"][:], in0=KW["rec"][:],
                                     scalar1=KW["lrneg"][:, n:n + 1],
                                     scalar2=None, op0=ALU.mult)
                vector.tensor_scalar(out=KW["ub"][:], in0=KW["recb"][:],
                                     scalar1=KW["lrneg"][:, m:m + 1],
                                     scalar2=None, op0=ALU.mult)
                vector.tensor_mul(KW["w"][:], KW["dI"][:], KW["rec"][:])
                vector.tensor_mul(KW["wb"][:], KW["dIb"][:], KW["recb"][:])
                for (nn, uu, ww) in [(n, "u", "w"), (m, "ub", "wb")]:
                    for (kR, kI, vR, vI, vRn) in pairs:
                        vector.scalar_tensor_tensor(
                            out=KW[kR][:], in0=KW[uu][:], scalar=rc(vR, nn),
                            in1=KW[kR][:], op0=ALU.mult, op1=ALU.add)
                        vector.scalar_tensor_tensor(
                            out=KW[kR + "w"][:], in0=KW[ww][:], scalar=rc(vI, nn),
                            in1=KW[kR + "w"][:], op0=ALU.mult, op1=ALU.add)
                        vector.scalar_tensor_tensor(
                            out=KW[kI][:], in0=KW[uu][:], scalar=rc(vI, nn),
                            in1=KW[kI][:], op0=ALU.mult, op1=ALU.add)
                        vector.scalar_tensor_tensor(
                            out=KW[kI + "w"][:], in0=KW[ww][:], scalar=rc(vRn, nn),
                            in1=KW[kI + "w"][:], op0=ALU.mult, op1=ALU.add)
            for (kR, kI, _v1, _v2, _v3) in pairs:
                vector.tensor_add(KW[kR][:], KW[kR][:], KW[kR + "w"][:])
                vector.tensor_add(KW[kI][:], KW[kI][:], KW[kI + "w"][:])
            vector.tensor_scalar_add(KW["t1"][:], KW["k11R"][:], 1.0)
            vector.tensor_mul(KW["p1"][:], KW["k01R"][:], KW["k10R"][:])
            vector.tensor_mul(KW["p2"][:], KW["k01I"][:], KW["k10I"][:])
            vector.tensor_mul(KW["t2"][:], KW["t1"][:], KW["t1"][:])
            vector.tensor_mul(KW["t3"][:], KW["k11I"][:], KW["k11I"][:])
            vector.tensor_sub(KW["t4"][:], KW["p1"][:], KW["p2"][:])
            vector.tensor_mul(KW["p3"][:], KW["k01R"][:], KW["k10I"][:])
            vector.tensor_mul(KW["p4"][:], KW["k01I"][:], KW["k10R"][:])
            vector.tensor_add(KW["den2"][:], KW["t2"][:], KW["t3"][:])
            vector.tensor_add(KW["sq"][:], KW["p3"][:], KW["p4"][:])
            vector.reciprocal(KW["rec"][:], KW["den2"][:])
            vpad()
            vector.tensor_mul(KW["u"][:], KW["t1"][:], KW["rec"][:])
            vector.tensor_mul(KW["w"][:], KW["k11I"][:], KW["rec"][:])
            vector.tensor_mul(KW["q1"][:], KW["t4"][:], KW["u"][:])
            vector.tensor_mul(KW["q2"][:], KW["sq"][:], KW["u"][:])
            vector.tensor_mul(KW["q3"][:], KW["sq"][:], KW["w"][:])
            vector.tensor_mul(KW["q4"][:], KW["t4"][:], KW["w"][:])
            vector.tensor_add(KW["p1"][:], KW["q1"][:], KW["q3"][:])
            vector.tensor_sub(KW["p2"][:], KW["q2"][:], KW["q4"][:])
            vector.tensor_sub(KW["nR"][:], KW["k00R"][:], KW["p1"][:])
            vector.tensor_sub(KW["nI"][:], KW["k00I"][:], KW["p2"][:])
            vector.tensor_mul(KW["x1"][:], KW["T"][:], KW["nI"][:])
            vector.tensor_mul(KW["x2"][:], KW["T"][:], KW["nR"][:])
            vector.tensor_sub(KW["arR"][:], KW["nR"][:], KW["x1"][:])
            kv.inc(vector.tensor_add(KW["arI"][:], KW["nI"][:], KW["x2"][:]))
            vector.wait_ge(kt.sem, 6)
            vector.tensor_copy(out=KW["SR"][:], in_=kp_a)
            vector.tensor_copy(out=KW["SI"][:], in_=kp_b)
            vector.tensor_mul(KW["t1"][:], KW["SR"][:], KW["Wc"][:])
            vector.tensor_mul(KW["t2"][:], KW["SI"][:], KW["Ws"][:])
            vector.tensor_mul(KW["t3"][:], KW["SR"][:], KW["Ws"][:])
            vector.tensor_mul(KW["t4"][:], KW["SI"][:], KW["Wc"][:])
            kv.inc(vector.tensor_sub(KW["PR"][:], KW["t1"][:], KW["t2"][:]))
            kv.inc(vector.tensor_add(KW["PI_"][:], KW["t3"][:], KW["t4"][:]))
            vector.wait_ge(kt.sem, 7)
            vector.tensor_scalar_mul(KW["kf"][:], kp_a, 1.0 / LL)
            vpad()
            kv.inc(vector.tensor_copy(out=s_kb[:], in_=KW["kf"][:]))
            vector.wait_ge(kd.sem, 32)
            kv.inc(vector.scalar_tensor_tensor(
                out=s_k2t[:], in0=s_kr1[:], scalar=16.0, in1=s_kr2[:],
                op0=ALU.mult, op1=ALU.add))
        ve(_k_vector)

        def _k_sync(sync):
            sync.wait_ge(kq.sem, 1)
            sync.wait_ge(kv.sem, 10)
            for buf in (kbuf, k2buf):
                for base in range(0, 8192, 4096):
                    ko.inc(sync.dma_start(
                        out=AP(tensor=buf.tensor, offset=base,
                               ap=[[64, 64], [1, 64]]),
                        in_=s_zt[:]))
            sync.wait_ge(ko.sem, 32)
            ko.inc(sync.dma_start(
                out=AP(tensor=kbuf.tensor, offset=4094, ap=[[64, 64], [1, 64]]),
                in_=s_kb[:]))
            sync.wait_ge(ko.sem, 80)
            kd.inc(sync.dma_start(
                out=s_kr1[:],
                in_=AP(tensor=kbuf.tensor, offset=4094, ap=[[64, 64], [1, 64]])))
            kd.inc(sync.dma_start(
                out=s_kr2[:],
                in_=AP(tensor=kbuf.tensor, offset=4093, ap=[[64, 64], [1, 64]])))
            sync.wait_ge(kv.sem, 11)
            ko.inc(sync.dma_start(
                out=AP(tensor=k2buf.tensor, offset=4094, ap=[[64, 64], [1, 64]]),
                in_=s_k2t[:]))
        sy(_k_sync)
        MS["k_done_v"] = kv.v
        MS["k_done_t"] = kt.v
        MS["k_done_o"] = ko.v

        cvt_jobs = [(sgb, sg_pack, None), (sub, su_pack, None),
                    (sdb, sd_nat, None), (zgb, zg_rev, sg_rev),
                    (zub, zu_rev, su_rev), (zdb, zd_nat, sd_nat)]
        prev_store = [0]
        for (dst, sa_, sb_) in cvt_jobs:
            for h in range(4):
                cols = slice(1376 * h, 1376 * h + 1376)

                def _ld(sync, sa_=sa_, sb_=sb_, cols=cols, pv=prev_store[0]):
                    if pv:
                        WAIT(sync, pv)
                    cW.inc(sync.dma_start(out=cvtA[:], in_=sa_[:, cols]))
                    if sb_ is not None:
                        cW.inc(sync.dma_start(out=cvtB[:], in_=sb_[:, cols]))
                sy(_ld)
                ld_ms = cW.ms()

                def _cv(vector, sb_=sb_, ld_ms=ld_ms):
                    WAIT(vector, ld_ms)
                    if sb_ is None:
                        kv.inc(vector.tensor_copy(out=cvtO[:], in_=cvtA[:]))
                    else:
                        kv.inc(vector.scalar_tensor_tensor(
                            out=cvtO[:], in0=cvtA[:], scalar=-1.0, in1=cvtB[:],
                            op0=ALU.mult, op1=ALU.mult))
                ve(_cv)
                cv_ms = kv.v

                def _st(sync, dst=dst, h=h, cv_ms=cv_ms):
                    sync.wait_ge(kv.sem, cv_ms)
                    dv = AP(tensor=dst.tensor, offset=1376 * h,
                            ap=[[5504, 128], [1, 1376]])
                    cW.inc(sync.dma_start(out=dv, in_=cvtO[:]))
                sy(_st)
                prev_store[0] = cW.ms()
        MS["cvt_done_w"] = cW.ms()
        MS["cvt_done_v"] = kv.v

        PH0_UPTO = 99

        def emit_ph0(p):
            def _pats(sync, p=p):
                if p == 0:
                    sync.wait_ge(kt.sem, MS["k_done_t"])
                    sync.wait_ge(kv.sem, MS["cvt_done_v"])
                else:
                    sync.wait_ge(cT.sem, MS[f"p{p-1}_ph2_T"])
                    sync.wait_ge(cV.sem, MS[f"p{p-1}_ph2_V"])
                    sync.wait_ge(cS.sem, MS[f"p{p-1}_ph2_S"])
                    sync.wait_ge(cO.sem, MS[f"p{p-1}_ph2_O"])
                cW.inc(sync.dma_start(
                    out=a_pat1.rearrange("p (c f) -> p c f", c=NCH),
                    in_=pat1_in.rearrange("c p f -> p c f")))
                cW.inc(sync.dma_start(
                    out=a_pat17.rearrange("p (c f) -> p c f", c=NCH),
                    in_=pat17_in.rearrange("c p f -> p c f")))
            sy(_pats)
            MS[f"p{p}_pats"] = cW.ms()

            xnb_ms, stat_ms, veps_ms = {}, {}, {}
            tr_ms, evac_ms = {}, {}
            for st in range(4):
                j = st % 2
                g = 4 * p + st
                rows = slice(SP * p + 128 * st, SP * p + 128 * st + 128)

                def _xr_guard(eng, st, p):
                    if st == 0:
                        if p == 0:
                            eng.wait_ge(kt.sem, MS["k_done_t"])
                            eng.wait_ge(kv.sem, MS["cvt_done_v"])
                            WAIT(eng, MS["cvt_done_w"])
                        else:
                            eng.wait_ge(cT.sem, MS[f"p{p-1}_ph2_T"])
                            eng.wait_ge(cV.sem, MS[f"p{p-1}_ph2_V"])
                            eng.wait_ge(cO.sem, MS[f"p{p-1}_ph2_O"])
                            WAIT(eng, MS[f"p{p-1}_ph2_W"])
                            eng.wait_ge(cG.sem, MS[f"p{p-1}_ph2_G"])
                    if st >= 2:
                        if (st - 2) in stat_ms:
                            eng.wait_ge(cV.sem, stat_ms[st - 2])
                        if (st - 2) in xnb_ms:
                            tag, vv = xnb_ms[st - 2]
                            eng.wait_ge(cS.sem if tag == "S" else cV.sem, vv)
                        if (st - 2) in tr_ms:
                            eng.wait_ge(cT.sem, tr_ms[st - 2])

                XSTRIPS = ((0, 2048), (2048, 3072), (3072, 4096))

                def _xrd(scalar, st=st, j=j, rows=rows, p=p):
                    _xr_guard(scalar, st, p)
                    c0, c1 = XSTRIPS[0]
                    cX.inc(scalar.dma_start(out=a_xr[j][0:128, c0:c1],
                                            in_=xr_in[rows, c0:c1]))
                sc(_xrd)

                def _xrd_sy(sync, st=st, j=j, rows=rows, p=p):
                    _xr_guard(sync, st, p)
                    c0, c1 = XSTRIPS[1]
                    cX.inc(sync.dma_start(out=a_xr[j][0:128, c0:c1],
                                          in_=xr_in[rows, c0:c1]))
                sy(_xrd_sy)

                def _xrd_gp(sync, st=st, j=j, rows=rows, p=p):
                    c0, c1 = XSTRIPS[2]
                    cX.inc(sync.dma_start(out=a_xr[j][0:128, c0:c1],
                                          in_=xr_in[rows, c0:c1]))
                sy(_xrd_gp)
                xr_ms = cX.v

                if PH0_UPTO < 1:
                    continue

                def _st_sc(scalar, j=j, st=st, xr_ms=xr_ms):
                    scalar.wait_ge(cX.sem, xr_ms)
                    if st >= 1:
                        scalar.wait_ge(cV.sem, veps_ms[st - 1])
                    cS.inc(scalar.activation(out=a_lnsq[:], in_=a_xr[j][:],
                                             func=AF.Square,
                                             accum_out=s_t[:, 1:2]))
                sc(_st_sc)
                sumsq_ms = cS.v

                def _st_v1(vector, j=j, xr_ms=xr_ms, sumsq_ms=sumsq_ms):
                    vector.wait_ge(cX.sem, xr_ms)
                    vector.reduce_sum(s_t[:, 0:1], a_xr[j][:], axis=AX.X)
                    vector.wait_ge(cS.sem, sumsq_ms)
                    vector.tensor_copy(out=s_t[:, 7:8], in_=s_t[:, 7:8])
                    vector.tensor_scalar_mul(s_t[:, 2:3], s_t[:, 0:1], 1.0 / 4096)
                    vector.tensor_scalar_mul(s_t[:, 4:5], s_t[:, 1:2], 1.0 / 4096)
                    vector.tensor_mul(s_t[:, 3:4], s_t[:, 2:3], s_t[:, 2:3])
                    vector.tensor_copy(out=s_t[:, 7:8], in_=s_t[:, 7:8])
                    vector.tensor_sub(s_t[:, 5:6], s_t[:, 4:5], s_t[:, 3:4])
                    vector.tensor_copy(out=s_t[:, 7:8], in_=s_t[:, 7:8])
                    cV.inc(vector.tensor_scalar_add(s_t[:, 6:7], s_t[:, 5:6],
                                                    1e-5))
                ve(_st_v1)
                veps_ms[st] = cV.v

                def _st_sd(scalar, st=st):
                    scalar.wait_ge(cV.sem, veps_ms[st])
                    if st >= 1:
                        scalar.wait_ge(cV.sem, stat_ms[st - 1])
                    cS.inc(scalar.activation(out=s_t[:, 8:9], in_=s_t[:, 6:7],
                                             func=AF.Sqrt))
                sc(_st_sd)
                sd_ms = cS.v

                def _st_v2(vector, g=g, sd_ms=sd_ms):
                    vector.wait_ge(cS.sem, sd_ms)
                    vector.reciprocal(s_stats[:, 3 * g:3 * g + 1], s_t[:, 8:9])
                    vector.tensor_copy(out=s_t[:, 7:8], in_=s_t[:, 7:8])
                    vector.tensor_mul(s_t[:, 9:10], s_t[:, 2:3],
                                      s_stats[:, 3 * g:3 * g + 1])
                    vector.tensor_copy(out=s_t[:, 7:8], in_=s_t[:, 7:8])
                    cV.inc(vector.tensor_scalar_mul(
                        s_stats[:, 3 * g + 1:3 * g + 2], s_t[:, 9:10], -1.0))
                ve(_st_v2)
                stat_ms[st] = cV.v

                if PH0_UPTO < 2:
                    continue

                def _xnbv(vector, j=j, g=g, st=st):
                    if st >= 2 and (st - 2) in tr_ms:
                        vector.wait_ge(cT.sem, tr_ms[st - 2])
                    vector.scalar_tensor_tensor(
                        out=a_xnb[j][:], in0=a_xr[j][:],
                        scalar=s_stats[:, 3 * g:3 * g + 1],
                        in1=a_xr[j][:], op0=ALU.mult, op1=ALU.bypass)
                    vector.tensor_copy(out=s_t[:, 7:8], in_=s_t[:, 7:8])
                    cV.inc(vector.tensor_scalar(
                        out=a_xnb[j][:], in0=a_xnb[j][:],
                        scalar1=s_stats[:, 3 * g + 1:3 * g + 2],
                        scalar2=None, op0=ALU.add))
                ve(_xnbv)
                xnb_ms[st] = ("V", cV.v)

                if PH0_UPTO < 3:
                    continue
                blist = [("xs", 0), ("xd", 0), ("xs", 1), ("xd", 1),
                         ("xs", 2), ("xd", 2), ("xs", 3), ("xd", 3)]
                for bi, (kind, r) in enumerate(blist):
                    half = bi % 2

                    def _tr(tensor, kind=kind, r=r, half=half, bi=bi, j=j, st=st):
                        if bi == 0:
                            tag, vv = xnb_ms[st]
                            tensor.wait_ge(cS.sem if tag == "S" else cV.sem, vv)
                        if bi >= 2:
                            tensor.wait_ge(cV.sem, evac_ms[(st, bi - 2)])
                        tb = p_trb if half == 0 else p_trc
                        ins = None
                        for q in range(4):
                            c = 4 * r + q
                            xnb = a_xnb[j]
                            xoff = 256 * c + (0 if kind == "xs" else 1)
                            src = AP(tensor=xnb.tensor,
                                     offset=xnb.offset + xoff,
                                     ap=[xnb.ap[0], [2, 128]])
                            ins = tensor.transpose(
                                tb[:, 128 * q:128 * q + 128],
                                src, s_id[:])
                        cT.inc(ins)
                    te(_tr)
                    tr_ms[st] = cT.v
                    trb_ms = cT.v

                    def _ev(vector, kind=kind, r=r, half=half, st=st,
                            trb_ms=trb_ms):
                        vector.wait_ge(cT.sem, trb_ms)
                        dstbuf = (xsT if kind == "xs" else xdT)[:]
                        tb = p_trb if half == 0 else p_trc
                        dst = AP(tensor=dstbuf.tensor,
                                 offset=dstbuf.offset + 512 * 4 * r + 128 * st,
                                 ap=[dstbuf.ap[0], [512, 4], [1, 128]])
                        cV.inc(vector.tensor_copy(
                            out=dst,
                            in_=tb[:, 0:512]
                            .rearrange("p (a b) -> p a b", a=4)))
                    ve(_ev)
                    evac_ms[(st, bi)] = cV.v
            MS[f"p{p}_ph0_lastevac"] = cV.v

            if PH0_UPTO < 4:
                return

            def _xsum(tensor, p=p):
                WAIT(tensor, MS[f"p{p}_pats"])
                tensor.wait_ge(cV.sem, MS[f"p{p}_ph0_lastevac"])
                ins = None
                for c in range(NCH):
                    tensor.matmul(p_xsum, a_pat1[:, 64 * c:64 * c + 64],
                                  xsT[:, 512 * c:512 * c + 512],
                                  start=(c == 0), stop=False, **SGC)
                    ins = tensor.matmul(p_xsum, a_pat1[:, 64 * c:64 * c + 64],
                                        xdT[:, 512 * c:512 * c + 512],
                                        start=False, stop=(c == NCH - 1), **SGC)
                cT.inc(ins)
            te(_xsum)
            MS[f"p{p}_xsum_T"] = cT.v

            def _xsev(vector, p=p):
                vector.wait_ge(cT.sem, MS[f"p{p}_xsum_T"])
                cV.inc(vector.tensor_copy(out=s_xsum[:], in_=p_xsum))
            ve(_xsev)
            MS[f"p{p}_ph0_V"] = cV.v
            MS[f"p{p}_ph0_S"] = cS.v
            MS[f"p{p}_ph0_T"] = cT.v

        def emit_ph1(p):
            if not ffn_on:
                for nm, cc_ in (("T", cT), ("V", cV), ("S", cS), ("G", cG),
                                ("W", cW)):
                    MS[f"p{p}_ph1_{nm}"] = (cc_.ms() if isinstance(cc_, MultiCtr) else cc_.v)
                return
            dma_ms, hiv_ms, wB_ms, wH_ms = {}, {}, {}, {}
            Tg_ms, Tu_ms, sig_ms, h1_ms, h2_ms = {}, {}, {}, {}, {}

            for it in range(NI):
                b = it % 2

                def _dma(sync, it=it, b=b, p=p):
                    if it == 0:
                        sync.wait_ge(cV.sem, MS[f"p{p}_ph0_V"])
                        sync.wait_ge(cS.sem, MS[f"p{p}_ph0_S"])
                        sync.wait_ge(cT.sem, MS[f"p{p}_ph0_T"])
                    if it >= 2:
                        sync.wait_ge(cV.sem, wB_ms[it - 2])
                        sync.wait_ge(cG.sem, wH_ms[it - 2])
                        sync.wait_ge(cT.sem, Tu_ms[it - 2])
                    for m, slab, scb in (("g", wg_slab, sgb), ("u", wu_slab, sub)):
                        cW.inc(sync.dma_start(out=b_wq[m][b][:], in_=slab[it]))
                        for jj in range(4):
                            off = (it * 4 + jj) * 2048
                            src = AP(tensor=scb.tensor, offset=off,
                                     ap=[[0, 32], [1, 2048]])
                            cW.inc(sync.dma_start(
                                out=b_sc[m][b][32 * jj:32 * jj + 32, :], in_=src))
                    for m, zb in (("g", zgb), ("u", zub)):
                        src = AP(tensor=zb.tensor, offset=128 * it,
                                 ap=[[INTER, 64], [1, 128]])
                        cW.inc(sync.dma_start(out=s_zs[m][b][:], in_=src))
                sy(_dma)
                dma_ms[it] = cW.ms()

                def _sig(scalar, it=it):
                    if it >= 1:
                        bb = (it - 1) % 2
                        scalar.wait_ge(cT.sem, Tg_ms[it - 1])
                        if it >= 3:
                            scalar.wait_ge(cV.sem, h2_ms[it - 3])
                        cS.inc(scalar.activation(
                            out=s_h1[bb][:],
                            in_=p_big[:, 512 * bb:512 * bb + 512],
                            func=AF.Silu))
                        sig_ms[it - 1] = cS.v
                sc(_sig)

                def _wB(vector, it=it, b=b):
                    WAIT(vector, dma_ms[it])
                    if it >= 2:
                        vector.wait_ge(cT.sem, Tu_ms[it - 2])
                        vector.wait_ge(cG.sem, wH_ms[it - 2])
                    vector.tensor_single_scalar(
                        out=b_hic["g"][b][:], in_=b_wq["g"][b][:],
                        scalar=4, op=ALU.logical_shift_right)
                    cV.inc(vector.tensor_single_scalar(
                        out=b_hic["u"][b][:], in_=b_wq["u"][b][:],
                        scalar=4, op=ALU.logical_shift_right))
                    hiv_ms[it] = cV.v
                    vector.tensor_single_scalar(
                        out=b_lo["g"][:], in_=b_wq["g"][b][:],
                        scalar=s_m15[:, 0:1], op=ALU.bitwise_and)
                    vector.tensor_single_scalar(
                        out=b_lo["u"][:], in_=b_wq["u"][b][:],
                        scalar=s_m15[:, 0:1], op=ALU.bitwise_and)
                    cV.inc(vector.tensor_tensor(
                        out=b_wB["g"][b][:], in0=b_lo["g"][:],
                        in1=b_sc["g"][b][:], op=ALU.mult))
                    if it >= 1:
                        bb = (it - 1) % 2
                        vector.wait_ge(cS.sem, sig_ms[it - 1])
                        vector.wait_ge(cT.sem, Tu_ms[it - 1])
                        cV.inc(vector.tensor_tensor(
                            out=h_act[:, 512 * (it - 1):512 * (it - 1) + 512],
                            in0=s_h1[bb][:],
                            in1=p_big[:, 1024 + 512 * bb:1024 + 512 * bb + 512],
                            op=ALU.mult))
                        h2_ms[it - 1] = cV.v
                    cV.inc(vector.tensor_tensor(
                        out=b_wB["u"][b][:], in0=b_lo["u"][:],
                        in1=b_sc["u"][b][:], op=ALU.mult))
                    wB_ms[it] = cV.v
                ve(_wB)

                def _wH(gpsimd, it=it, b=b):
                    gpsimd.wait_ge(cV.sem, hiv_ms[it])
                    WAIT(gpsimd, dma_ms[it])
                    if it >= 2:
                        gpsimd.wait_ge(cT.sem, Tu_ms[it - 2])
                    gpsimd.tensor_tensor(out=b_wH["g"][b][:], in0=b_hic["g"][b][:],
                                         in1=b_sc["g"][b][:], op=ALU.mult)
                    cG.inc(gpsimd.tensor_tensor(
                        out=b_wH["u"][b][:], in0=b_hic["u"][b][:],
                        in1=b_sc["u"][b][:], op=ALU.mult))
                gp(_wH)
                wH_ms[it] = cG.v

                if dbg and p == 1 and it == NI - 1:
                    def _dbgd(gpsimd, b=b):
                        gpsimd.wait_ge(cV.sem, wB_ms[NI - 1])
                        gpsimd.dma_start(out=dbg_wB, in_=b_wB["g"][b][:]).then_inc(dbsem, 16)
                        gpsimd.dma_start(out=dbg_wH, in_=b_wH["g"][b][:]).then_inc(dbsem, 16)
                        gpsimd.dma_start(out=dbg_sc, in_=b_sc["g"][b][:]).then_inc(dbsem, 16)
                        gpsimd.dma_start(out=dbg_wBu, in_=b_wB["u"][b][:]).then_inc(dbsem, 16)
                        gpsimd.dma_start(out=dbg_wHu, in_=b_wH["u"][b][:]).then_inc(dbsem, 16)
                        gpsimd.dma_start(out=dbg_scu, in_=b_sc["u"][b][:]).then_inc(dbsem, 16)
                        gpsimd.dma_start(out=dbg_zsu, in_=s_zs["u"][b][:]).then_inc(dbsem, 16)
                        gpsimd.wait_ge(dbsem, 112)
                    gp(_dbgd)

                def _mm(tensor, it=it, b=b):
                    tensor.wait_ge(cV.sem, wB_ms[it])
                    tensor.wait_ge(cG.sem, wH_ms[it])
                    if it >= 2:
                        tensor.wait_ge(cS.sem, sig_ms[it - 2])
                        tensor.wait_ge(cV.sem, h2_ms[it - 2])
                    for m, pso in (("g", 0), ("u", 1024)):
                        ps = p_big[:, pso + 512 * b:pso + 512 * b + 512]
                        for c in range(NCH):
                            tensor.matmul(ps,
                                          b_wB[m][b][:, 128 * c:128 * c + 128],
                                          xsT[:, 512 * c:512 * c + 512],
                                          start=(c == 0), stop=False, **SGC)
                        for c in range(NCH):
                            tensor.matmul(ps,
                                          b_wH[m][b][:, 128 * c:128 * c + 128],
                                          xdT[:, 512 * c:512 * c + 512],
                                          start=False, stop=False, **SGC)
                        cT.inc(tensor.matmul(ps, s_zs[m][b][:], s_xsum[:],
                                             start=False, stop=True, **SGC))
                        if m == "g":
                            Tg_ms[it] = cT.v
                        else:
                            Tu_ms[it] = cT.v
                te(_mm)

            def _tail_sig(scalar):
                bb = (NI - 1) % 2
                scalar.wait_ge(cT.sem, Tg_ms[NI - 1])
                cS.inc(scalar.activation(out=s_h1[bb][:],
                                         in_=p_big[:, 512 * bb:512 * bb + 512],
                                         func=AF.Silu))
                sig_ms[NI - 1] = cS.v
            sc(_tail_sig)

            def _tail_v(vector):
                bb = (NI - 1) % 2
                vector.wait_ge(cS.sem, sig_ms[NI - 1])
                vector.wait_ge(cT.sem, Tu_ms[NI - 1])
                cV.inc(vector.tensor_tensor(
                    out=h_act[:, 512 * (NI - 1):512 * NI],
                    in0=s_h1[bb][:],
                    in1=p_big[:, 1024 + 512 * bb:1024 + 512 * bb + 512],
                    op=ALU.mult))
            ve(_tail_v)
            for nm, cc_ in (("T", cT), ("V", cV), ("S", cS), ("G", cG),
                            ("W", cW)):
                MS[f"p{p}_ph1_{nm}"] = (cc_.ms() if isinstance(cc_, MultiCtr) else cc_.v)

        def emit_ph2(p):
            def _cload(sync, p=p):
                sync.wait_ge(cT.sem, MS[f"p{p}_ph1_T"])
                sync.wait_ge(cV.sem, MS[f"p{p}_ph1_V"])
                sync.wait_ge(cS.sem, MS[f"p{p}_ph1_S"])
                sync.wait_ge(cG.sem, MS[f"p{p}_ph1_G"])
                if p == 0:
                    sync.wait_ge(ko.sem, MS["k_done_o"])
                cW.inc(sync.dma_start(
                    out=c_qpA.rearrange("p (c f) -> p c f", c=32),
                    in_=qpatA_in.rearrange("c p f -> p c f")))
                cW.inc(sync.dma_start(
                    out=c_qpB.rearrange("p (c f) -> p c f", c=11),
                    in_=qpatB_in.rearrange("c p f -> p c f")))
                MS[f"p{p}_cload_qp"] = cW.ms()
                cW.inc(sync.dma_start(
                    out=c_md.rearrange("p (c f) -> p c f", c=NMC),
                    in_=AP(tensor=kbuf.tensor, offset=128 * 30,
                           ap=[[2, 128], [128, NMC], [1, 128]])))
                cW.inc(sync.dma_start(
                    out=c_ms.rearrange("p (c f) -> p c f", c=NMC),
                    in_=AP(tensor=kbuf.tensor, offset=128 * 30 - 1,
                           ap=[[2, 128], [128, NMC], [1, 128]])))
                cW.inc(sync.dma_start(
                    out=c_zdA[:], in_=AP(tensor=zdb.tensor, offset=0,
                                         ap=[[4096, 128], [1, 4096]])))
                cW.inc(sync.dma_start(
                    out=c_zdB[:], in_=AP(tensor=zdb.tensor, offset=128 * 4096,
                                         ap=[[4096, 44], [1, 4096]])))
            sy(_cload)
            MS[f"p{p}_cload"] = cW.ms()

            if ffn_on:
                def _hsum(tensor, p=p):
                    WAIT(tensor, MS[f"p{p}_cload_qp"])
                    tensor.wait_ge(cV.sem, MS[f"p{p}_ph1_V"])
                    fA = True
                    startedB = False
                    ins = None
                    for t in range(NI):
                        teff = t % NCC
                        rhs = h_act[:, 512 * t:512 * t + 512]
                        lastA = t == 43 + 31
                        lastB = t == NI - 1
                        if teff < 32:
                            ins = tensor.matmul(
                                p_hA, c_qpA[:, 128 * teff:128 * teff + 128],
                                rhs, start=fA, stop=lastA, **SGC)
                            fA = False
                        else:
                            ins = tensor.matmul(
                                p_hB,
                                c_qpB[:, 44 * (teff - 32):44 * (teff - 32) + 44],
                                rhs, start=not startedB, stop=lastB, **SGC)
                            startedB = True
                    cT.inc(ins)
                te(_hsum)
                MS[f"p{p}_hsum_T"] = cT.v

                def _hsev(vector, p=p):
                    vector.wait_ge(cT.sem, MS[f"p{p}_hsum_T"])
                    vector.tensor_copy(out=s_hsumA[:], in_=p_hA)
                    cV.inc(vector.tensor_copy(out=s_hsumB[0:44, :], in_=p_hB))
                ve(_hsev)
                MS[f"p{p}_hd_V"] = cV.v
            else:
                MS[f"p{p}_hd_V"] = cV.v

            dd_ms, hid_ms, wBd_ms, wHd_ms, mmd_ms = {}, {}, {}, {}, {}
            bank_ms, evd_ms, asm_ms = {}, {}, {}
            t1_ms, outv_ms, odma_ms, xn_ms = {}, {}, {}, {}
            step = [0]
            for g4 in range(NG4):
                if ffn_on:
                    for cc in range(NCC):
                        stp = step[0]
                        q = stp % 2

                        def _ddma(sync, g4=g4, cc=cc, q=q, stp=stp):
                            if stp >= 2:
                                sync.wait_ge(cV.sem, wBd_ms[stp - 2])
                                sync.wait_ge(cG.sem, wHd_ms[stp - 2])
                            cW.inc(sync.dma_start(
                                out=c_wqd[q][:],
                                in_=wd_slab[cc][:, 512 * g4:512 * g4 + 512]))
                            for jj in range(4):
                                src = AP(tensor=sdb.tensor,
                                         offset=(4 * cc + jj) * 4096 + 512 * g4,
                                         ap=[[0, 32], [1, 512]])
                                cW.inc(sync.dma_start(
                                    out=c_scd[q][32 * jj:32 * jj + 32, :],
                                    in_=src))
                        sy(_ddma)
                        dd_ms[stp] = cW.ms()

                        def _dwB(vector, q=q, stp=stp):
                            WAIT(vector, dd_ms[stp])
                            if stp >= 2:
                                vector.wait_ge(cT.sem, mmd_ms[stp - 2])
                                vector.wait_ge(cG.sem, wHd_ms[stp - 2])
                            cV.inc(vector.tensor_single_scalar(
                                out=c_hic[q][:], in_=c_wqd[q][:],
                                scalar=4,
                                op=ALU.logical_shift_right))
                            hid_ms[stp] = cV.v
                            vector.tensor_single_scalar(
                                out=c_lo[:], in_=c_wqd[q][:],
                                scalar=s_m15[:, 0:1], op=ALU.bitwise_and)
                            vector.tensor_copy(out=s_t[:, 7:8], in_=s_t[:, 7:8])
                            cV.inc(vector.tensor_tensor(
                                out=c_wBd[q][:], in0=c_lo[:],
                                in1=c_scd[q][:], op=ALU.mult))
                        ve(_dwB)
                        wBd_ms[stp] = cV.v

                        def _dwH(gpsimd, q=q, stp=stp):
                            gpsimd.wait_ge(cV.sem, hid_ms[stp])
                            WAIT(gpsimd, dd_ms[stp])
                            if stp >= 2:
                                gpsimd.wait_ge(cT.sem, mmd_ms[stp - 2])
                            cG.inc(gpsimd.tensor_tensor(
                                out=c_wHd[q][:], in0=c_hic[q][:],
                                in1=c_scd[q][:], op=ALU.mult))
                        gp(_dwH)
                        wHd_ms[stp] = cG.v

                        def _dmm(tensor, g4=g4, cc=cc, q=q, stp=stp, p=p):
                            tensor.wait_ge(cV.sem, wBd_ms[stp])
                            tensor.wait_ge(cG.sem, wHd_ms[stp])
                            if cc == 0:
                                tensor.wait_ge(cV.sem, MS[f"p{p}_hd_V"])
                                WAIT(tensor, MS[f"p{p}_cload"])
                                if g4 >= 1:
                                    tensor.wait_ge(cS.sem, evd_ms[(g4 - 1, 3)])
                            ins = None
                            for jj in range(4):
                                ps = p_big[:, 512 * jj:512 * jj + 512]
                                tensor.matmul(
                                    ps, c_wBd[q][:, 128 * jj:128 * jj + 128],
                                    h_act[:, 512 * (43 + cc):512 * (43 + cc) + 512],
                                    start=(cc == 0), stop=False, **SGC)
                                ins = tensor.matmul(
                                    ps, c_wHd[q][:, 128 * jj:128 * jj + 128],
                                    h_act[:, 512 * cc:512 * cc + 512],
                                    start=False, stop=False, **SGC)
                            cT.inc(ins)
                        te(_dmm)
                        mmd_ms[stp] = cT.v
                        step[0] += 1

                def _closemm(tensor, g4=g4, p=p):
                    if not ffn_on:
                        WAIT(tensor, MS[f"p{p}_cload"])
                        tensor.wait_ge(cV.sem, MS[f"p{p}_ph0_V"])
                        if g4 >= 1:
                            tensor.wait_ge(cS.sem, evd_ms[(g4 - 1, 3)])
                    for jj in range(4):
                        tau = 4 * g4 + jj
                        ps = p_big[:, 512 * jj:512 * jj + 512]
                        ops = []
                        if ssm_on:
                            for c in range(NCH):
                                u = tau + 2 * c
                                if u < 30:
                                    continue
                                uo = 128 * (u - 30)
                                ops.append((c_ms[:, uo:uo + 128],
                                            xsT[:, 512 * c:512 * c + 512]))
                                ops.append((c_md[:, uo:uo + 128],
                                            xdT[:, 512 * c:512 * c + 512]))
                        if ffn_on:
                            ops.append((c_zdA[:, 128 * tau:128 * tau + 128],
                                        s_hsumA[:]))
                            ops.append((c_zdB[:, 128 * tau:128 * tau + 128],
                                        s_hsumB[0:44, :]))
                        ins = None
                        for oi, (lh, rh) in enumerate(ops):
                            ins = tensor.matmul(
                                ps, lh, rh,
                                start=(not ffn_on and oi == 0),
                                stop=(oi == len(ops) - 1), **SGC)
                        cT.inc(ins)
                        bank_ms[(g4, jj)] = cT.v
                te(_closemm)

                for jj in range(4):
                    def _evd(scalar, g4=g4, jj=jj):
                        scalar.wait_ge(cT.sem, bank_ms[(g4, jj)])
                        if g4 >= 1 and jj == 0:
                            scalar.wait_ge(cT.sem, asm_ms[(g4 - 1, 3)])
                        cS.inc(scalar.activation(
                            out=c_yT[jj][:],
                            in_=p_big[:, 512 * jj:512 * jj + 512],
                            func=AF.Identity))
                    sc(_evd)
                    evd_ms[(g4, jj)] = cS.v

                for kpair in (0, 2):
                  for k in (kpair, kpair + 1):
                    def _trb(tensor, g4=g4, k=k):
                        tensor.wait_ge(cS.sem, evd_ms[(g4, 3)])
                        prev = (g4, k - 2) if k >= 2 else (g4 - 1, k + 2)
                        if prev in outv_ms:
                            tensor.wait_ge(cV.sem, outv_ms[prev])
                        tb = p_trb if k % 2 == 0 else p_trc
                        ins = None
                        for jj in range(4):
                            ins = tensor.transpose(
                                tb[:, 128 * jj:128 * jj + 128],
                                c_yT[jj][:, 128 * k:128 * k + 128], s_id[:])
                        cT.inc(ins)
                    te(_trb)
                    asm_ms[(g4, k)] = cT.v

                    def _xadma(scalar, g4=g4, k=k, p=p):
                        kk = k % 2
                        prev = (g4, k - 2) if k >= 2 else (g4 - 1, k + 2)
                        if prev in t1_ms:
                            scalar.wait_ge(cV.sem, t1_ms[prev])
                        rows = slice(SP * p + 128 * k, SP * p + 128 * k + 128)
                        cols = slice(512 * g4, 512 * g4 + 512)
                        cX.inc(scalar.dma_start(out=c_xa[kk][:],
                                                in_=x_in[rows, cols]))
                        xa_v = cX.v
                        g = 4 * p + k
                        scalar.wait_ge(cX.sem, xa_v)
                        cS.inc(scalar.activation(
                            out=c_xna[kk][:], in_=c_xa[kk][:], func=AF.Identity,
                            scale=s_stats[:, 3 * g:3 * g + 1],
                            bias=s_stats[:, 3 * g + 1:3 * g + 2]))
                        xn_ms[(g4, k)] = cS.v
                    sc(_xadma)

                  k = kpair
                  def _asm(vector, g4=g4, k=k):
                        for k2 in (k, k + 1):
                            kk = k2 % 2
                            vector.wait_ge(cS.sem, xn_ms[(g4, k2)])
                            prev = (g4, k2 - 2) if k2 >= 2 else (g4 - 1, k2 + 2)
                            if prev in odma_ms:
                                vector.wait_ge(cO.sem, odma_ms[prev])
                            cV.inc(vector.scalar_tensor_tensor(
                                out=c_oa[kk][:], in0=c_xna[kk][:],
                                scalar=s_dvec[:, 0:1], in1=c_xa[kk][:],
                                op0=ALU.mult, op1=ALU.add))
                            t1_ms[(g4, k2)] = cV.v
                        for k2 in (k, k + 1):
                            kk = k2 % 2
                            vector.wait_ge(cT.sem, asm_ms[(g4, k2)])
                            src = (p_trb if k2 % 2 == 0 else p_trc)[:, 0:512]
                            cV.inc(vector.tensor_tensor(
                                out=c_oa[kk][:], in0=c_oa[kk][:],
                                in1=src, op=ALU.add))
                            outv_ms[(g4, k2)] = cV.v
                  ve(_asm)

                  def _odma(scalar, g4=g4, k=k, p=p):
                        for k2 in (k, k + 1):
                            kk = k2 % 2
                            scalar.wait_ge(cV.sem, outv_ms[(g4, k2)])
                            rows = slice(SP * p + 128 * k2,
                                         SP * p + 128 * k2 + 128)
                            cols = slice(512 * g4, 512 * g4 + 512)
                            cO.inc(scalar.dma_start(out=y_out[rows, cols],
                                                    in_=c_oa[kk][:]))
                            odma_ms[(g4, k2)] = cO.v
                  sc(_odma)

            for nm, cc_ in (("T", cT), ("V", cV), ("S", cS), ("G", cG),
                            ("W", cW), ("O", cO)):
                MS[f"p{p}_ph2_{nm}"] = (cc_.ms() if isinstance(cc_, MultiCtr) else cc_.v)

        stage = 0
        for p in range(NPAN):
            if stage >= upto:
                break
            emit_ph0(p); stage += 1
            if stage >= upto:
                break
            emit_ph1(p); stage += 1
            if stage >= upto:
                break
            emit_ph2(p); stage += 1

        final_o = cO.v

        def _fin(scalar):
            scalar.wait_ge(cO.sem, final_o)
        sc(_fin)

    return nc


def _host_prep(inputs):
    u = np.ascontiguousarray(np.asarray(inputs["L0_attn_u"]).reshape(8192, 4096))
    perm = np.concatenate([np.arange(0, INTER, 2), np.arange(1, INTER, 2)])

    def u8view(a):
        a = np.ascontiguousarray(np.asarray(a))
        return a.view(np.uint8).reshape(a.shape[0], a.shape[1], 4)[:, :, 0]

    def gu_slab(wq):
        A = u8view(wq)[:, ::-1][perm]
        B = A.reshape(NI, 128, NCH, 128).transpose(0, 3, 2, 1)
        return np.ascontiguousarray(B.reshape(NI, 128, 2048))

    wg = gu_slab(inputs["L0_ffn_gate_Wq"])
    wu = gu_slab(inputs["L0_ffn_up_Wq"])
    wd = np.ascontiguousarray(
        u8view(inputs["L0_ffn_down_Wq"]).T.reshape(NCC, 128, 4096))

    def rev64(s):
        return np.ascontiguousarray(np.asarray(s)[perm].T[::-1]).astype(f32)

    sgr = rev64(inputs["L0_ffn_gate_scale"])
    sur = rev64(inputs["L0_ffn_up_scale"])
    zgr = rev64(inputs["L0_ffn_gate_zero"])
    zur = rev64(inputs["L0_ffn_up_zero"])

    def pack_gu(s):
        t = s.reshape(64, NI, 128)
        t = t.reshape(NCH, 4, NI, 128)
        t = t.transpose(2, 1, 0, 3)
        return np.ascontiguousarray(t.reshape(128, 5504))

    sg_pack = pack_gu(sgr)
    su_pack = pack_gu(sur)
    sdn = np.ascontiguousarray(
        np.asarray(inputs["L0_ffn_down_scale"]).T).astype(f32)
    zdn = np.ascontiguousarray(
        np.asarray(inputs["L0_ffn_down_zero"]).T).astype(f32)

    flat128 = lambda a: np.ascontiguousarray(a.reshape(128, 5504))

    pat1 = np.zeros((NCH, 128, 64), dtype=ml_dtypes.bfloat16)
    pat17 = np.zeros((NCH, 128, 64), dtype=ml_dtypes.bfloat16)
    for c in range(NCH):
        for pp in range(128):
            pat1[c, pp, 4 * c + pp // 32] = 1
            pat17[c, pp, 4 * c + pp // 32] = 17
    qpatA = np.zeros((32, 128, 128), dtype=ml_dtypes.bfloat16)
    qpatB = np.zeros((11, 128, 44), dtype=ml_dtypes.bfloat16)
    for t in range(32):
        for rr in range(128):
            qpatA[t, rr, 4 * t + rr // 32] = 1
    for t in range(11):
        for rr in range(128):
            qpatB[t, rr, 4 * (t + 32) - 128 + rr // 32] = 1
    ident = np.eye(128, dtype=ml_dtypes.bfloat16)

    a = np.arange(64)
    prod = a[:, None] * a[None, :]
    fold = lambda v, m: ((v + m // 2) % m) - m // 2
    lam = np.asarray(inputs["L0_attn_Lambda_ri"])
    pri = np.asarray(inputs["L0_attn_p_ri"])
    qri = np.asarray(inputs["L0_attn_q_ri"])
    vc = np.asarray(inputs["L0_attn_Vc_ri"])
    ct = np.asarray(inputs["L0_attn_Ct_ri"])
    common = {
        "wg_slab": wg, "wu_slab": wu, "wd_slab": wd,
        "sg_rev": flat128(sgr), "su_rev": flat128(sur),
        "zg_rev": flat128(zgr), "zu_rev": flat128(zur),
        "sg_pack": sg_pack, "su_pack": su_pack,
        "sd_nat": flat128(sdn), "zd_nat": flat128(zdn),
        "pat1": pat1, "pat17": pat17, "qpatA": qpatA, "qpatB": qpatB,
        "ident": ident,
        "dvec": np.broadcast_to(
            np.asarray(inputs["L0_attn_D"]).astype(f32).reshape(1, 1),
            (128, 1)).copy(),
        "i_lam_r": np.ascontiguousarray(lam[:, 0]).reshape(1, 64).astype(f32),
        "i_lam_i": np.ascontiguousarray(lam[:, 1]).reshape(1, 64).astype(f32),
        "i_p_r": np.ascontiguousarray(pri[:, 0]).reshape(1, 64).astype(f32),
        "i_p_i": np.ascontiguousarray(pri[:, 1]).reshape(1, 64).astype(f32),
        "i_q_r": np.ascontiguousarray(qri[:, 0]).reshape(1, 64).astype(f32),
        "i_q_i": np.ascontiguousarray(qri[:, 1]).reshape(1, 64).astype(f32),
        "i_ct_r": np.ascontiguousarray(ct[0, :, 0]).reshape(1, 64).astype(f32),
        "i_ct_i": np.ascontiguousarray(ct[0, :, 1]).reshape(1, 64).astype(f32),
        "i_vcr_t": np.ascontiguousarray(vc[:, :, 0].T).astype(f32),
        "i_vci_t": np.ascontiguousarray(vc[:, :, 1].T).astype(f32),
        "i_bp": np.asarray(inputs["L0_attn_Bp"]).astype(f32),
        "i_log_step": np.asarray(
            inputs["L0_attn_log_step"]).reshape(1, 1).astype(f32),
        "i_lramp": (64 * a[:, None] + a[None, :]).astype(f32),
        "i_re": fold(prod, 64).astype(f32),
        "i_re2": fold(prod + 16, 64).astype(f32),
        "i_rw": fold(prod, 4096).astype(f32),
        "i_rw2": fold(prod + 1024, 4096).astype(f32),
    }
    in_maps = []
    for c in range(NCORE):
        d = dict(common)
        xc = np.ascontiguousarray(u[ROWS * c:ROWS * (c + 1)])
        d["x"] = xc
        d["xr"] = np.ascontiguousarray(xc[:, ::-1])
        in_maps.append(d)
    return in_maps


_NC_CACHE = {}


def kernel(**inputs):
    if "full" not in _NC_CACHE:
        _NC_CACHE["full"] = build_program(ssm_on=True, ffn_on=True)
    nc = _NC_CACHE["full"]
    in_maps = _host_prep(inputs)
    res = run_bass_kernel_spmd(nc, in_maps, core_ids=list(range(NCORE)))
    y = np.concatenate([r["y"] for r in res.results], axis=0)
    return y.reshape(1, 8192, 4096)
